# version 18
# baseline (speedup 1.0000x reference)
import numpy as np
from contextlib import ExitStack

import concourse.bass as bass
import concourse.mybir as mybir
from concourse.bass_utils import run_bass_kernel_spmd

F32 = mybir.dt.float32
F16 = mybir.dt.float16
I32 = mybir.dt.int32
AF = mybir.ActivationFunctionType
ALU = mybir.AluOpType
AX = mybir.AxisListType

S = 2048
D = 1024
NB = 4
BT = 512
NT = 4
ND = 8
NE = 16
NI = 32
NO = 8
NW2U = 32
NUNITS = {"whg": NE, "w1": NI, "w2": NW2U}
MMKEY = {"whg": "hg", "w1": "f1", "w2": "f2u"}
WFN = {"whg": 2, "w1": 3, "w2": 3}
EPS = 1e-5


class Ctr:

    def __init__(self):
        self.n = 0
        self.at = {}

    def ev(self, key, k=1):
        self.n += k
        self.at[key] = self.n
        return self.n


def plan():
    mm = Ctr()
    ve = Ctr()
    sc = Ctr()
    gp = Ctr()

    qwt = {}
    for name, nu in NUNITS.items():
        for u in range(nu):
            for h in range(2):
                g = 2 * u + h
                qwt[(name, u, h)] = 16 * (g // WFN[name] + 1)

    def tr_events(b):
        for k in range(NT):
            for h in range(2):
                mm.ev(("tr", b, k, h))

    tr_events(0)
    for b in range(NB):
        for j in range(NE):
            mm.ev(("hg", b, j))
        for i in range(NI):
            mm.ev(("f1", b, i))
        for u in range(NW2U):
            mm.ev(("f2u", b, u))
        for k in range(NT):
            for h in range(2):
                mm.ev(("ytr", b, k, h))
        if b + 1 < NB:
            tr_events(b + 1)

    ALLU = []
    for name, nu in NUNITS.items():
        for u in range(nu):
            ALLU.append((name, u))
    NALLU = len(ALLU)

    def anchors(b):
        out = []
        for kk in range(ND):
            out.append(("mix", kk, 2 * kk + 6))
        for i in range(NI):
            out.append(("h1", i, NE + i + 5))
        for o in range(NO):
            out.append(("f2", o, NE + NI + 4 * (o + 1) + 5))
        if b > 0:
            out = [(t, p, 0) for (t, p, _) in out]
        return out

    scw = {}
    cnt3 = {}
    for name, u in ALLU:
        s = u % 3
        cnt3[(name, s)] = cnt3.get((name, s), 0) + 1
        scw[(name, u)] = 32 * cnt3[(name, s)]

    th3 = {}
    cnt3b = {}
    for b in range(1, NB):
        for name, nu in NUNITS.items():
            for u in range(nu):
                s = (b * nu + u) % 3
                cnt3b[(name, s)] = cnt3b.get((name, s), 0) + 1
                th3[(name, b, u)] = 32 * cnt3b[(name, s)]

    for b in range(NB):
        for tag, pl, tgt in anchors(b):
            if tag == "mix":
                sc.ev(("z", b, pl))
                sc.ev(("sigh", b, pl))
            elif tag == "h1":
                sc.ev(("h1", b, pl))
            else:
                sc.ev(("sf", b, pl))

    ve.ev(("init",))
    cur_ve = [0]

    def ve_split(target):
        while cur_ve[0] < min(target, NALLU):
            name, u = ALLU[cur_ve[0]]
            for h in range(2):
                ve.ev(("whi", name, u, h))
                ve.ev(("wlo", name, u, h))
            cur_ve[0] += 1

    def ln_events(b):
        for k in range(NT):
            ve.ev(("xn", b, k))

    ln_events(0)
    for b in range(NB):
        for k in range(NT):
            for h in range(2):
                ve.ev(("evac", b, k, h))
            if b == 0:
                ve_split(k)
        for tag, pl, tgt in anchors(b):
            if tag == "mix":
                ve.ev(("lin", b, pl))
                ve.ev(("scan", b, pl))
            elif tag == "h1":
                ve.ev(("h1lo", b, pl))
                if pl == NI - 1 and b + 1 < NB:
                    ln_events(b + 1)
            else:
                ve.ev(("fb", b, pl))
                ve.ev(("g2", b, pl))
                ve.ev(("y", b, pl))
            ve_split(tgt)
        for k in range(NT):
            ve.ev(("res", b, k))

    return mm, ve, sc, gp, qwt, ALLU, anchors, scw, th3


def build_nc():
    mm, ve, sc, gp, qwt, ALLU, anchors, SCW, TH3 = plan()
    MM, VE, SC, QWT = mm.at, ve.at, sc.at, qwt
    NALLU = len(ALLU)

    nc = bass.Bass("TRN2", target_bir_lowering=False,
                   detect_race_conditions=False)

    x_hbm = nc.dram_tensor("x", [S, D], F32, kind="ExternalInput").ap()
    whg_hbm = nc.dram_tensor("whg", [2 * NE, 128, 512], F32, kind="ExternalInput").ap()
    w1_hbm = nc.dram_tensor("w1", [2 * NI, 128, 512], F32, kind="ExternalInput").ap()
    w2_hbm = nc.dram_tensor("w2", [2 * NW2U, 128, 512], F32, kind="ExternalInput").ap()
    b1_hbm = nc.dram_tensor("b1t", [128, NI], F32, kind="ExternalInput").ap()
    b2_hbm = nc.dram_tensor("b2t", [128, NO], F32, kind="ExternalInput").ap()
    id_hbm = nc.dram_tensor("ident", [128, 128], F32, kind="ExternalInput").ap()
    out_hbm = nc.dram_tensor("out", [S, D], F32, kind="ExternalOutput").ap()
    whbm = {"whg": whg_hbm, "w1": w1_hbm, "w2": w2_hbm}
    scr_hi = {n: nc.dram_tensor(f"scr_hi_{n}", [nu, 128, 1024], F16).ap()
              for n, nu in NUNITS.items()}
    scr_lo = {n: nc.dram_tensor(f"scr_lo_{n}", [nu, 128, 1024], F16).ap()
              for n, nu in NUNITS.items()}

    es = ExitStack()
    with es:
        def sb(name, shape, dt=F32):
            return es.enter_context(nc.sbuf_tensor(name, shape, dt))

        def ps(name, shape):
            return es.enter_context(nc.psum_tensor(name, shape, F32))

        xt = [[sb(f"xt{p}_{k}", [128, D]) for k in range(NT)] for p in range(2)]
        xn_tm = [sb(f"xntm{k}", [128, D]) for k in range(NT)]
        xnhi = [sb(f"xnhi{d}", [128, BT], F16) for d in range(ND)]
        xnlo = [sb(f"xnlo{d}", [128, BT], F16) for d in range(ND)]
        h1hi = [sb(f"h1hi{i}", [128, BT], F16) for i in range(NI)]
        h1lo = [sb(f"h1lo{i}", [128, BT], F16) for i in range(NI)]
        h1f = sb("h1f", [128, BT])
        s_t = [sb(f"s{d}", [128, BT]) for d in range(ND)]
        zbuf = sb("zbuf", [128, BT])
        gbuf = sb("gbuf", [128, BT])
        lbuf = sb("lbuf", [128, BT])
        vbuf = sb("vbuf", [128, BT])
        sfb = lbuf
        g2b = vbuf
        wf = {n: [sb(f"wf_{n}{s}", [128, 512]) for s in range(WFN[n])]
              for n in NUNITS}
        whi = {n: [sb(f"whi_{n}{p}", [128, 1024], F16) for p in range(3)]
               for n in NUNITS}
        wlo = {n: [sb(f"wlo_{n}{p}", [128, 1024], F16) for p in range(3)]
               for n in NUNITS}
        stats = sb("stats", [128, NT * 8])
        carry = sb("carry", [128, ND])
        neghalf = sb("neghalf", [128, 1])
        b1t = sb("b1t_s", [128, NI])
        b2t = sb("b2t_s", [128, NO])
        ident = sb("ident_s", [128, 128])

        pm = [ps("pm0", [128, BT]), ps("pm1", [128, BT])]
        pt = [ps("pt0", [128, BT]), ps("pt1", [128, BT])]
        py = [ps("py0", [128, BT]), ps("py1", [128, BT])]

        s_mm = es.enter_context(nc.semaphore("s_mm"))
        s_ve = es.enter_context(nc.semaphore("s_ve"))
        s_sc = es.enter_context(nc.semaphore("s_sc"))
        s_gp = es.enter_context(nc.semaphore("s_gp"))
        q_x = es.enter_context(nc.semaphore("q_x"))
        q_c = es.enter_context(nc.semaphore("q_c"))
        q_xk = [es.enter_context(nc.semaphore(f"q_xk{k}")) for k in range(NT)]
        q_w = {
            name: [es.enter_context(nc.semaphore(f"qw_{name}{s}"))
                   for s in range(WFN[name])]
            for name in NUNITS
        }
        q_scrw = {
            name: [es.enter_context(nc.semaphore(f"qsw_{name}{s}")) for s in range(3)]
            for name in NUNITS
        }
        q_w3 = {
            name: [es.enter_context(nc.semaphore(f"qw3_{name}{s}")) for s in range(3)]
            for name in NUNITS
        }
        block = es.enter_context(nc.Block())


        def grp_free(eng, b, seq):
            p = seq - 2
            if p < 0:
                return
            if p < NE:
                kk = p // 2
                if p % 2 == 0:
                    eng.wait_ge(s_sc, SC[("z", b, kk)])
                else:
                    eng.wait_ge(s_ve, VE[("lin", b, kk)])
            elif p < NE + NI:
                eng.wait_ge(s_sc, SC[("h1", b, p - NE)])
            else:
                eng.wait_ge(s_ve, VE[("fb", b, p - NE - NI)])

        @block.sync
        def _(sync):
            def wdma(name, u):
                for h in range(2):
                    g = 2 * u + h
                    if g >= WFN[name]:
                        pu, ph = divmod(g - WFN[name], 2)
                        sync.wait_ge(s_ve, VE[("wlo", name, pu, ph)])
                    sync.dma_start(
                        out=wf[name][g % WFN[name]][:], in_=whbm[name][g]
                    ).then_inc(q_w[name][g % WFN[name]], 16)

            for name in NUNITS:
                wdma(name, 0)
            for name, u in ALLU:
                if u == 0:
                    continue
                wdma(name, u)
            for b in range(1, NB):
                for name, u in ALLU:
                    nu = NUNITS[name]
                    g = b * nu + u
                    pb, pu = divmod(g - 3, nu)
                    sync.wait_ge(s_mm, MM[(MMKEY[name], pb, pu)])
                    if b == 1:
                        sync.wait_ge(q_scrw[name][u % 3], SCW[(name, u)])
                    sync.dma_start(
                        out=whi[name][g % 3][:], in_=scr_hi[name][u]
                    ).then_inc(q_w3[name][g % 3], 16)
                    sync.dma_start(
                        out=wlo[name][g % 3][:], in_=scr_lo[name][u]
                    ).then_inc(q_w3[name][g % 3], 16)

        @block.gpsimd
        def _(gpsimd):
            gpsimd.dma_start(out=ident[:], in_=id_hbm[:]).then_inc(q_c, 16)
            gpsimd.dma_start(out=b1t[:], in_=b1_hbm[:]).then_inc(q_c, 16)
            gpsimd.dma_start(out=b2t[:], in_=b2_hbm[:]).then_inc(q_c, 16)

            def xload(b, k):
                t0 = b * BT + k * 128
                gpsimd.dma_start(
                    out=xt[b % 2][k][:], in_=x_hbm[t0:t0 + 128, :]
                ).then_inc(q_xk[k], 16)

            for b in (0, 1):
                for k in range(NT):
                    xload(b, k)
            for name, u in ALLU:
                g3 = u % 3
                gpsimd.wait_ge(s_ve, VE[("wlo", name, u, 1)])
                gpsimd.dma_start(
                    out=scr_hi[name][u], in_=whi[name][g3][:]
                ).then_inc(q_scrw[name][g3], 16)
                gpsimd.dma_start(
                    out=scr_lo[name][u], in_=wlo[name][g3][:]
                ).then_inc(q_scrw[name][g3], 16)
            for b in range(NB):
                for k in range(NT):
                    t0 = b * BT + k * 128
                    gpsimd.wait_ge(s_ve, VE[("res", b, k)])
                    gpsimd.dma_start(
                        out=out_hbm[t0:t0 + 128, :], in_=xt[b % 2][k][:]
                    ).then_inc(q_x, 16)
                if b + 2 < NB:
                    for k in range(NT):
                        xload(b + 2, k)
            gpsimd.wait_ge(q_x, 16 * NB * NT)

        @block.tensor
        def _(tensor):
            tensor.wait_ge(q_c, 48)

            def tr_group(b):
                for k in range(NT):
                    slot = k
                    tensor.wait_ge(s_ve, VE[("xn", b, k)])
                    for h in range(2):
                        if k >= 1:
                            tensor.wait_ge(s_ve, VE[("evac", b, k - 1, h)])
                        elif b >= 1:
                            tensor.wait_ge(s_ve, VE[("evac", b - 1, NT - 1, h)])
                        last = None
                        for dd in range(4):
                            d = h * 4 + dd
                            last = tensor.matmul(
                                pt[h][:, dd * 128:(dd + 1) * 128],
                                xn_tm[slot][:, d * 128:(d + 1) * 128],
                                ident[:], start=True, stop=True,
                            )
                        last.then_inc(s_mm, 1)

            def passes3(bank, w_hi, w_lo, x_hi, x_lo, first, final):
                n = 0
                last = None
                for a, bb in ((w_hi, x_hi), (w_hi, x_lo), (w_lo, x_hi)):
                    last = tensor.matmul(
                        bank[:], a, bb,
                        start=(first and n == 0), stop=(final and n == 2),
                    )
                    n += 1
                return last

            tr_group(0)
            for b in range(NB):
                seq = 0
                for j in range(NE):
                    if j == 0:
                        tensor.wait_ge(s_ve, VE[("evac", b, NT - 1, 1)])
                    grp_free(tensor, b, seq)
                    bank = pm[seq % 2]
                    last = None
                    for d in range(ND):
                        if d in (0, 4):
                            h = d // 4
                            if b == 0:
                                tensor.wait_ge(s_ve, VE[("wlo", "whg", j, h)])
                            elif h == 0:
                                tensor.wait_ge(
                                    q_w3["whg"][(b * NE + j) % 3],
                                    TH3[("whg", b, j)],
                                )
                        g3 = (b * NE + j) % 3
                        last = passes3(
                            bank,
                            whi["whg"][g3][:, d * 128:(d + 1) * 128],
                            wlo["whg"][g3][:, d * 128:(d + 1) * 128],
                            xnhi[d][:], xnlo[d][:],
                            first=(d == 0), final=(d == ND - 1),
                        )
                    last.then_inc(s_mm, 1)
                    seq += 1
                for i in range(NI):
                    grp_free(tensor, b, seq)
                    bank = pm[seq % 2]
                    last = None
                    for d in range(ND):
                        if d in (0, 4):
                            h = d // 4
                            if b == 0:
                                tensor.wait_ge(s_ve, VE[("wlo", "w1", i, h)])
                            elif h == 0:
                                tensor.wait_ge(
                                    q_w3["w1"][(b * NI + i) % 3],
                                    TH3[("w1", b, i)],
                                )
                        g3 = (b * NI + i) % 3
                        last = passes3(
                            bank,
                            whi["w1"][g3][:, d * 128:(d + 1) * 128],
                            wlo["w1"][g3][:, d * 128:(d + 1) * 128],
                            xnhi[d][:], xnlo[d][:],
                            first=(d == 0), final=(d == ND - 1),
                        )
                    last.then_inc(s_mm, 1)
                    seq += 1
                for o in range(NO):
                    grp_free(tensor, b, seq)
                    bank = pm[seq % 2]
                    for u4 in range(4):
                        u = o * 4 + u4
                        tensor.wait_ge(s_sc, SC[("h1", b, u4 * 8 + 7)])
                        tensor.wait_ge(s_ve, VE[("h1lo", b, u4 * 8 + 7)])
                        last = None
                        for ii in range(8):
                            i = u4 * 8 + ii
                            if ii in (0, 4):
                                h = ii // 4
                                if b == 0:
                                    tensor.wait_ge(s_ve, VE[("wlo", "w2", u, h)])
                                elif h == 0:
                                    tensor.wait_ge(
                                        q_w3["w2"][(b * NW2U + u) % 3],
                                        TH3[("w2", b, u)],
                                    )
                            g3 = (b * NW2U + u) % 3
                            last = passes3(
                                bank,
                                whi["w2"][g3][:, ii * 128:(ii + 1) * 128],
                                wlo["w2"][g3][:, ii * 128:(ii + 1) * 128],
                                h1hi[i][:], h1lo[i][:],
                                first=(i == 0), final=(i == NI - 1),
                            )
                        last.then_inc(s_mm, 1)
                    seq += 1
                for k in range(NT):
                    for h in range(2):
                        tensor.wait_ge(s_ve, VE[("y", b, h * 4 + 3)])
                        if k >= 1:
                            tensor.wait_ge(s_ve, VE[("res", b, k - 1)])
                        elif b >= 1:
                            tensor.wait_ge(s_ve, VE[("res", b - 1, NT - 1)])
                        last = None
                        for dd in range(4):
                            d = h * 4 + dd
                            last = tensor.matmul(
                                py[h][:, dd * 128:(dd + 1) * 128],
                                s_t[d][:, k * 128:(k + 1) * 128],
                                ident[:], start=True, stop=True,
                            )
                        last.then_inc(s_mm, 1)
                if b + 1 < NB:
                    tr_group(b + 1)

        @block.scalar
        def _(scalar):
            scalar.wait_ge(q_c, 48)
            for b in range(NB):
                for tag, pl, tgt in anchors(b):
                    if tag == "mix":
                        kk = pl
                        if kk >= 1:
                            scalar.wait_ge(s_ve, VE[("scan", b, kk - 1)])
                        elif b > 0:
                            scalar.wait_ge(s_ve, VE[("scan", b - 1, ND - 1)])
                        scalar.wait_ge(s_mm, MM[("hg", b, 2 * kk)])
                        gate_bank = pm[(2 * kk) % 2]
                        scalar.activation(
                            out=zbuf[:], in_=gate_bank[:], func=AF.Sigmoid,
                        ).then_inc(s_sc, 1)
                        scalar.wait_ge(s_ve, VE[("lin", b, kk)])
                        scalar.activation(
                            out=gbuf[:], in_=lbuf[:], func=AF.Sigmoid,
                            bias=neghalf[:], scale=1.0,
                        ).then_inc(s_sc, 1)
                    elif tag == "h1":
                        i = pl
                        scalar.wait_ge(s_mm, MM[("f1", b, i)])
                        bank = pm[(NE + i) % 2]
                        scalar.activation(
                            out=h1hi[i][:], in_=bank[:], func=AF.Relu,
                            bias=b1t[:, i:i + 1], scale=1.0,
                        )
                        if i >= 1:
                            scalar.wait_ge(s_ve, VE[("h1lo", b, i - 1)])
                        elif b > 0:
                            scalar.wait_ge(s_ve, VE[("h1lo", b - 1, NI - 1)])
                        scalar.activation(
                            out=h1f[:], in_=bank[:], func=AF.Relu,
                            bias=b1t[:, i:i + 1], scale=1.0,
                        ).then_inc(s_sc, 1)
                    else:
                        o = pl
                        scalar.wait_ge(s_ve, VE[("fb", b, o)])
                        scalar.activation(
                            out=sfb[:], in_=g2b[:], func=AF.Sigmoid,
                        ).then_inc(s_sc, 1)

        @block.vector
        def _(vector):
            def vdrain():
                vector.drain()

            vector.memset(carry[:], 0.0)
            vector.memset(neghalf[:], -0.5).then_inc(s_ve, 1)

            cur = [0]

            def ve_split(target):
                while cur[0] < min(target, NALLU):
                    name, u = ALLU[cur[0]]
                    for h in range(2):
                        g = 2 * u + h
                        vector.wait_ge(
                            q_w[name][g % WFN[name]], QWT[(name, u, h)]
                        )
                        if h == 0 and u >= 3:
                            vector.wait_ge(s_mm, MM[(MMKEY[name], 0, u - 3)])
                            vector.wait_ge(
                                q_scrw[name][u % 3], SCW[(name, u - 3)]
                            )
                        vector.tensor_copy(
                            whi[name][u % 3][:, h * 512:(h + 1) * 512],
                            wf[name][g % WFN[name]][:],
                        ).then_inc(s_ve, 1)
                        vdrain()
                        vector.tensor_tensor(
                            wlo[name][u % 3][:, h * 512:(h + 1) * 512],
                            wf[name][g % WFN[name]][:],
                            whi[name][u % 3][:, h * 512:(h + 1) * 512],
                            ALU.subtract,
                        ).then_inc(s_ve, 1)
                    cur[0] += 1

            def col4(c):
                return stats[:, c:c + 25:8]

            def ln_vector(b):
                for k in range(NT):
                    vector.wait_ge(q_xk[k], 16 * (b + 1))
                    if b >= 1:
                        vector.wait_ge(s_mm, MM[("tr", b - 1, k, 1)])
                    base = k * 8
                    vector.reduce_sum(
                        stats[:, base:base + 1], xt[b % 2][k][:], axis=AX.X
                    )
                    vector.scalar_tensor_tensor(
                        xn_tm[k][:], xt[b % 2][k][:], 1.0, xt[b % 2][k][:],
                        ALU.mult, ALU.mult,
                        accum_out=stats[:, base + 1:base + 2],
                    )
                vdrain()
                su, sq_ = col4(0), col4(1)
                mean, ex2 = col4(2), col4(3)
                w, rstd, tmp = col4(4), col4(6), col4(7)
                vector.tensor_scalar_mul(mean, su, 1.0 / D)
                vector.tensor_scalar_mul(ex2, sq_, 1.0 / D)
                vdrain()
                vector.tensor_tensor(tmp, mean, mean, ALU.mult)
                vdrain()
                vector.tensor_sub(w, ex2, tmp)
                vdrain()
                vector.tensor_scalar_add(w, w, EPS)
                vdrain()
                vector.reciprocal(rstd, w)
                vdrain()
                for _ in range(7):
                    vector.tensor_mul(tmp, w, rstd)
                    vdrain()
                    vector.tensor_mul(tmp, tmp, rstd)
                    vdrain()
                    vector.tensor_scalar(tmp, tmp, -0.5, 1.5, ALU.mult, ALU.add)
                    vdrain()
                    vector.tensor_mul(rstd, rstd, tmp)
                    vdrain()
                for k in range(NT):
                    base = k * 8
                    vector.tensor_scalar(
                        xn_tm[k][:], xt[b % 2][k][:],
                        stats[:, base + 2:base + 3], stats[:, base + 6:base + 7],
                        ALU.subtract, ALU.mult,
                    ).then_inc(s_ve, 1)

            ln_vector(0)
            for b in range(NB):
                for k in range(NT):
                    for h in range(2):
                        vector.wait_ge(s_mm, MM[("tr", b, k, h)])
                        for dd in range(4):
                            d = h * 4 + dd
                            vector.tensor_copy(
                                xnhi[d][:, k * 128:(k + 1) * 128],
                                pt[h][:, dd * 128:(dd + 1) * 128],
                            )
                        vdrain()
                        last = None
                        for dd in range(4):
                            d = h * 4 + dd
                            last = vector.tensor_tensor(
                                xnlo[d][:, k * 128:(k + 1) * 128],
                                pt[h][:, dd * 128:(dd + 1) * 128],
                                xnhi[d][:, k * 128:(k + 1) * 128],
                                ALU.subtract,
                            )
                        last.then_inc(s_ve, 1)
                    if b == 0:
                        ve_split(k)
                for tag, pl, tgt in anchors(b):
                    if tag == "mix":
                        kk = pl
                        if b > 0 and kk == 0:
                            vector.wait_ge(s_mm, MM[("ytr", b - 1, NT - 1, 1)])
                        vector.wait_ge(s_mm, MM[("hg", b, 2 * kk + 1)])
                        hid_bank = pm[(2 * kk + 1) % 2]
                        vector.tensor_scalar_add(
                            lbuf[:], hid_bank[:], 0.5
                        ).then_inc(s_ve, 1)
                        vector.wait_ge(s_sc, SC[("sigh", b, kk)])
                        vdrain()
                        vector.tensor_max(lbuf[:], lbuf[:], gbuf[:])
                        vector.wait_ge(s_sc, SC[("z", b, kk)])
                        vdrain()
                        vector.tensor_mul(vbuf[:], zbuf[:], lbuf[:])
                        vector.tensor_scalar(
                            zbuf[:], zbuf[:], -1.0, 1.0, ALU.mult, ALU.add
                        )
                        vdrain()
                        vector.tensor_tensor_scan(
                            s_t[kk][:], zbuf[:], vbuf[:],
                            carry[:, kk:kk + 1], ALU.mult, ALU.add,
                        )
                        vdrain()
                        vector.tensor_copy(
                            carry[:, kk:kk + 1], s_t[kk][:, BT - 1:BT]
                        ).then_inc(s_ve, 1)
                    elif tag == "h1":
                        i = pl
                        vector.wait_ge(s_sc, SC[("h1", b, i)])
                        vector.tensor_tensor(
                            h1lo[i][:], h1f[:], h1hi[i][:], ALU.subtract
                        ).then_inc(s_ve, 1)
                        if i == NI - 1 and b + 1 < NB:
                            ln_vector(b + 1)
                    else:
                        o = pl
                        vector.wait_ge(s_mm, MM[("f2u", b, o * 4 + 3)])
                        bank = pm[(NE + NI + o) % 2]
                        vector.tensor_scalar_add(
                            g2b[:], bank[:], b2t[:, o:o + 1]
                        ).then_inc(s_ve, 1)
                        vector.wait_ge(s_sc, SC[("sf", b, o)])
                        vdrain()
                        vector.tensor_mul(
                            g2b[:], g2b[:], sfb[:]
                        ).then_inc(s_ve, 1)
                        vdrain()
                        vector.tensor_mul(
                            s_t[o][:], s_t[o][:], g2b[:]
                        ).then_inc(s_ve, 1)
                    ve_split(tgt)
                for k in range(NT):
                    last = None
                    for h in range(2):
                        vector.wait_ge(s_mm, MM[("ytr", b, k, h)])
                        last = vector.tensor_tensor(
                            xt[b % 2][k][:, h * 512:(h + 1) * 512],
                            xt[b % 2][k][:, h * 512:(h + 1) * 512],
                            py[h][:], ALU.add,
                        )
                    last.then_inc(s_ve, 1)

    return nc


_NC_CACHE = None


def kernel(L0_attn_x, L0_attn_W_hg, L0_ffn_W1, L0_ffn_b1, L0_ffn_W2, L0_ffn_b2):
    global _NC_CACHE
    x = np.ascontiguousarray(np.asarray(L0_attn_x, dtype=np.float32))
    whg = np.asarray(L0_attn_W_hg, dtype=np.float32)
    w1 = np.asarray(L0_ffn_W1, dtype=np.float32)
    b1 = np.asarray(L0_ffn_b1, dtype=np.float32)
    w2 = np.asarray(L0_ffn_W2, dtype=np.float32)
    b2 = np.asarray(L0_ffn_b2, dtype=np.float32)

    def halves(units):
        n = units.shape[0]
        return np.ascontiguousarray(
            units.reshape(n, 128, 2, 512).transpose(0, 2, 1, 3).reshape(2 * n, 128, 512)
        )

    e_order = []
    for kk in range(8):
        e_order.extend([8 + kk, kk])
    whg_t = whg.reshape(8, 128, 16, 128).transpose(2, 1, 0, 3)[e_order]
    whg_t = halves(whg_t.reshape(16, 128, 1024))
    w1_t = halves(
        w1.reshape(8, 128, 32, 128).transpose(2, 1, 0, 3).reshape(32, 128, 1024)
    )
    w2_t = halves(
        w2.reshape(4, 8, 128, 8, 128).transpose(3, 0, 2, 1, 4).reshape(32, 128, 1024)
    )
    b1_t = np.ascontiguousarray(b1.reshape(32, 128).T)
    b2_t = np.ascontiguousarray(b2.reshape(8, 128).T)
    ident = np.eye(128, dtype=np.float32)

    if _NC_CACHE is None:
        _NC_CACHE = build_nc()
    nc = _NC_CACHE

    in_maps = [
        {
            "x": np.ascontiguousarray(x[c]),
            "whg": whg_t,
            "w1": w1_t,
            "w2": w2_t,
            "b1t": b1_t,
            "b2t": b2_t,
            "ident": ident,
        }
        for c in range(8)
    ]
    res = run_bass_kernel_spmd(nc, in_maps, core_ids=list(range(8)))
    return np.stack([r["out"] for r in res.results], axis=0)


# version 19
# speedup vs baseline: 1.0281x; 1.0281x over previous
import numpy as np
from contextlib import ExitStack

import concourse.bass as bass
import concourse.mybir as mybir
from concourse.bass_utils import run_bass_kernel_spmd

F32 = mybir.dt.float32
F16 = mybir.dt.float16
I32 = mybir.dt.int32
AF = mybir.ActivationFunctionType
ALU = mybir.AluOpType
AX = mybir.AxisListType

S = 2048
D = 1024
NB = 4
BT = 512
NT = 4
ND = 8
NE = 16
NI = 32
NO = 8
NW2U = 32
NUNITS = {"whg": NE, "w1": NI, "w2": NW2U}
MMKEY = {"whg": "hg", "w1": "f1", "w2": "f2u"}
WFN = {"whg": 2, "w1": 3, "w2": 3}
EPS = 1e-5


class Ctr:

    def __init__(self):
        self.n = 0
        self.at = {}

    def ev(self, key, k=1):
        self.n += k
        self.at[key] = self.n
        return self.n


def plan():
    mm = Ctr()
    ve = Ctr()
    sc = Ctr()
    gp = Ctr()

    qwt = {}
    for name, nu in NUNITS.items():
        for u in range(nu):
            for h in range(2):
                g = 2 * u + h
                qwt[(name, u, h)] = 16 * (g // WFN[name] + 1)

    def tr_events(b):
        for k in range(NT):
            for h in range(2):
                mm.ev(("tr", b, k, h))

    tr_events(0)
    for b in range(NB):
        for j in range(NE):
            mm.ev(("hg", b, j))
        for i in range(NI):
            mm.ev(("f1", b, i))
        for u in range(NW2U):
            mm.ev(("f2u", b, u))
        for k in range(NT):
            for h in range(2):
                mm.ev(("ytr", b, k, h))
        if b + 1 < NB:
            tr_events(b + 1)

    ALLU = []
    for name, nu in NUNITS.items():
        for u in range(nu):
            ALLU.append((name, u))
    NALLU = len(ALLU)

    def anchors(b):
        out = []
        for kk in range(ND):
            out.append(("mix", kk, 2 * kk + 6))
        for i in range(NI):
            out.append(("h1", i, NE + i + 5))
        for o in range(NO):
            out.append(("f2", o, NE + NI + 4 * (o + 1) + 5))
        if b > 0:
            out = [(t, p, 0) for (t, p, _) in out]
        return out

    scw = {}
    cnt3 = {}
    for name, u in ALLU:
        s = u % 3
        cnt3[(name, s)] = cnt3.get((name, s), 0) + 1
        scw[(name, u)] = 32 * cnt3[(name, s)]

    th3 = {}
    cnt3b = {}
    for b in range(1, NB):
        for name, nu in NUNITS.items():
            for u in range(nu):
                s = (b * nu + u) % 3
                cnt3b[(name, s)] = cnt3b.get((name, s), 0) + 1
                th3[(name, b, u)] = 32 * cnt3b[(name, s)]

    for b in range(NB):
        for tag, pl, tgt in anchors(b):
            if tag == "mix":
                sc.ev(("z", b, pl))
                sc.ev(("sigh", b, pl))
            elif tag == "h1":
                sc.ev(("h1", b, pl))
            else:
                sc.ev(("sf", b, pl))

    ve.ev(("init",))
    cur_ve = [0]

    def ve_split(target):
        while cur_ve[0] < min(target, NALLU):
            name, u = ALLU[cur_ve[0]]
            for h in range(2):
                ve.ev(("whi", name, u, h))
                ve.ev(("wlo", name, u, h))
            cur_ve[0] += 1

    def ln_events(b):
        for k in range(NT):
            ve.ev(("xn", b, k))

    ln_events(0)
    for b in range(NB):
        for k in range(NT):
            for h in range(2):
                ve.ev(("evac", b, k, h))
            if b == 0:
                ve_split(k)
        for tag, pl, tgt in anchors(b):
            if tag == "mix":
                ve.ev(("lin", b, pl))
                ve.ev(("scan", b, pl))
            elif tag == "h1":
                ve.ev(("h1lo", b, pl))
                if pl == NI - 1 and b + 1 < NB:
                    ln_events(b + 1)
            else:
                ve.ev(("fb", b, pl))
                ve.ev(("g2", b, pl))
                ve.ev(("y", b, pl))
            ve_split(tgt)
        for k in range(NT):
            ve.ev(("res", b, k))

    return mm, ve, sc, gp, qwt, ALLU, anchors, scw, th3


def build_nc():
    mm, ve, sc, gp, qwt, ALLU, anchors, SCW, TH3 = plan()
    MM, VE, SC, QWT = mm.at, ve.at, sc.at, qwt
    NALLU = len(ALLU)

    nc = bass.Bass("TRN2", target_bir_lowering=False,
                   detect_race_conditions=False)

    x_hbm = nc.dram_tensor("x", [S, D], F32, kind="ExternalInput").ap()
    whg_hbm = nc.dram_tensor("whg", [2 * NE, 128, 512], F32, kind="ExternalInput").ap()
    w1_hbm = nc.dram_tensor("w1", [2 * NI, 128, 512], F32, kind="ExternalInput").ap()
    w2_hbm = nc.dram_tensor("w2", [2 * NW2U, 128, 512], F32, kind="ExternalInput").ap()
    b1_hbm = nc.dram_tensor("b1t", [128, NI], F32, kind="ExternalInput").ap()
    b2_hbm = nc.dram_tensor("b2t", [128, NO], F32, kind="ExternalInput").ap()
    id_hbm = nc.dram_tensor("ident", [128, 128], F32, kind="ExternalInput").ap()
    out_hbm = nc.dram_tensor("out", [S, D], F32, kind="ExternalOutput").ap()
    whbm = {"whg": whg_hbm, "w1": w1_hbm, "w2": w2_hbm}
    scr_hi = {n: nc.dram_tensor(f"scr_hi_{n}", [nu, 128, 1024], F16).ap()
              for n, nu in NUNITS.items()}
    scr_lo = {n: nc.dram_tensor(f"scr_lo_{n}", [nu, 128, 1024], F16).ap()
              for n, nu in NUNITS.items()}

    es = ExitStack()
    with es:
        def sb(name, shape, dt=F32):
            return es.enter_context(nc.sbuf_tensor(name, shape, dt))

        def ps(name, shape):
            return es.enter_context(nc.psum_tensor(name, shape, F32))

        xt = [[sb(f"xt{p}_{k}", [128, D]) for k in range(NT)] for p in range(2)]
        xn_tm = [sb(f"xntm{k}", [128, D]) for k in range(NT)]
        xnhi = [sb(f"xnhi{d}", [128, BT], F16) for d in range(ND)]
        xnlo = [sb(f"xnlo{d}", [128, BT], F16) for d in range(ND)]
        h1hi = [sb(f"h1hi{i}", [128, BT], F16) for i in range(NI)]
        h1lo = [sb(f"h1lo{i}", [128, BT], F16) for i in range(NI)]
        h1f = sb("h1f", [128, BT])
        s_t = [sb(f"s{d}", [128, BT]) for d in range(ND)]
        zbuf = sb("zbuf", [128, BT])
        gbuf = sb("gbuf", [128, BT])
        lbuf = sb("lbuf", [128, BT])
        vbuf = sb("vbuf", [128, BT])
        sfb = lbuf
        g2b = vbuf
        wf = {n: [sb(f"wf_{n}{s}", [128, 512]) for s in range(WFN[n])]
              for n in NUNITS}
        whi = {n: [sb(f"whi_{n}{p}", [128, 1024], F16) for p in range(3)]
               for n in NUNITS}
        wlo = {n: [sb(f"wlo_{n}{p}", [128, 1024], F16) for p in range(3)]
               for n in NUNITS}
        stats = sb("stats", [128, NT * 8])
        carry = sb("carry", [128, ND])
        neghalf = sb("neghalf", [128, 1])
        b1t = sb("b1t_s", [128, NI])
        b2t = sb("b2t_s", [128, NO])
        ident = sb("ident_s", [128, 128])

        pm = [ps("pm0", [128, BT]), ps("pm1", [128, BT])]
        pt = [ps("pt0", [128, BT]), ps("pt1", [128, BT])]
        py = [ps("py0", [128, BT]), ps("py1", [128, BT])]

        s_mm = es.enter_context(nc.semaphore("s_mm"))
        s_ve = es.enter_context(nc.semaphore("s_ve"))
        s_sc = es.enter_context(nc.semaphore("s_sc"))
        s_gp = es.enter_context(nc.semaphore("s_gp"))
        q_x = es.enter_context(nc.semaphore("q_x"))
        q_c = es.enter_context(nc.semaphore("q_c"))
        q_xk = [es.enter_context(nc.semaphore(f"q_xk{k}")) for k in range(NT)]
        q_w = {
            name: [es.enter_context(nc.semaphore(f"qw_{name}{s}"))
                   for s in range(WFN[name])]
            for name in NUNITS
        }
        q_scrw = {
            name: [es.enter_context(nc.semaphore(f"qsw_{name}{s}")) for s in range(3)]
            for name in NUNITS
        }
        q_w3 = {
            name: [es.enter_context(nc.semaphore(f"qw3_{name}{s}")) for s in range(3)]
            for name in NUNITS
        }
        block = es.enter_context(nc.Block())


        def grp_free(eng, b, seq):
            p = seq - 2
            if p < 0:
                return
            if p < NE:
                kk = p // 2
                if p % 2 == 0:
                    eng.wait_ge(s_sc, SC[("z", b, kk)])
                else:
                    eng.wait_ge(s_ve, VE[("lin", b, kk)])
            elif p < NE + NI:
                eng.wait_ge(s_sc, SC[("h1", b, p - NE)])
            else:
                eng.wait_ge(s_ve, VE[("fb", b, p - NE - NI)])

        @block.sync
        def _(sync):
            def wdma(name, u):
                for h in range(2):
                    g = 2 * u + h
                    if g >= WFN[name]:
                        pu, ph = divmod(g - WFN[name], 2)
                        sync.wait_ge(s_ve, VE[("wlo", name, pu, ph)])
                    sync.dma_start(
                        out=wf[name][g % WFN[name]][:], in_=whbm[name][g]
                    ).then_inc(q_w[name][g % WFN[name]], 16)

            sync.wait_ge(q_xk[NT - 1], 16)
            for name in NUNITS:
                wdma(name, 0)
            for name, u in ALLU:
                if u == 0:
                    continue
                wdma(name, u)
            for b in range(1, NB):
                for name, u in ALLU:
                    nu = NUNITS[name]
                    g = b * nu + u
                    pb, pu = divmod(g - 3, nu)
                    sync.wait_ge(s_mm, MM[(MMKEY[name], pb, pu)])
                    if b == 1:
                        sync.wait_ge(q_scrw[name][u % 3], SCW[(name, u)])
                    sync.dma_start(
                        out=whi[name][g % 3][:], in_=scr_hi[name][u]
                    ).then_inc(q_w3[name][g % 3], 16)
                    sync.dma_start(
                        out=wlo[name][g % 3][:], in_=scr_lo[name][u]
                    ).then_inc(q_w3[name][g % 3], 16)

        @block.gpsimd
        def _(gpsimd):
            gpsimd.dma_start(out=ident[:], in_=id_hbm[:]).then_inc(q_c, 16)
            gpsimd.dma_start(out=b1t[:], in_=b1_hbm[:]).then_inc(q_c, 16)
            gpsimd.dma_start(out=b2t[:], in_=b2_hbm[:]).then_inc(q_c, 16)

            def xload(b, k):
                t0 = b * BT + k * 128
                gpsimd.dma_start(
                    out=xt[b % 2][k][:], in_=x_hbm[t0:t0 + 128, :]
                ).then_inc(q_xk[k], 16)

            for k in range(NT):
                xload(0, k)
            gpsimd.wait_ge(q_xk[NT - 1], 16)
            for k in range(NT):
                xload(1, k)
            for name, u in ALLU:
                g3 = u % 3
                gpsimd.wait_ge(s_ve, VE[("wlo", name, u, 1)])
                gpsimd.dma_start(
                    out=scr_hi[name][u], in_=whi[name][g3][:]
                ).then_inc(q_scrw[name][g3], 16)
                gpsimd.dma_start(
                    out=scr_lo[name][u], in_=wlo[name][g3][:]
                ).then_inc(q_scrw[name][g3], 16)
            for b in range(NB):
                for k in range(NT):
                    t0 = b * BT + k * 128
                    gpsimd.wait_ge(s_ve, VE[("res", b, k)])
                    gpsimd.dma_start(
                        out=out_hbm[t0:t0 + 128, :], in_=xt[b % 2][k][:]
                    ).then_inc(q_x, 16)
                if b + 2 < NB:
                    for k in range(NT):
                        xload(b + 2, k)
            gpsimd.wait_ge(q_x, 16 * NB * NT)

        @block.tensor
        def _(tensor):
            tensor.wait_ge(q_c, 48)

            def tr_group(b):
                for k in range(NT):
                    slot = k
                    tensor.wait_ge(s_ve, VE[("xn", b, k)])
                    for h in range(2):
                        if k >= 1:
                            tensor.wait_ge(s_ve, VE[("evac", b, k - 1, h)])
                        elif b >= 1:
                            tensor.wait_ge(s_ve, VE[("evac", b - 1, NT - 1, h)])
                        last = None
                        for dd in range(4):
                            d = h * 4 + dd
                            last = tensor.matmul(
                                pt[h][:, dd * 128:(dd + 1) * 128],
                                xn_tm[slot][:, d * 128:(d + 1) * 128],
                                ident[:], start=True, stop=True,
                            )
                        last.then_inc(s_mm, 1)

            def passes3(bank, w_hi, w_lo, x_hi, x_lo, first, final):
                n = 0
                last = None
                for a, bb in ((w_hi, x_hi), (w_hi, x_lo), (w_lo, x_hi)):
                    last = tensor.matmul(
                        bank[:], a, bb,
                        start=(first and n == 0), stop=(final and n == 2),
                    )
                    n += 1
                return last

            tr_group(0)
            for b in range(NB):
                seq = 0
                for j in range(NE):
                    if j == 0:
                        tensor.wait_ge(s_ve, VE[("evac", b, NT - 1, 1)])
                    grp_free(tensor, b, seq)
                    bank = pm[seq % 2]
                    last = None
                    for d in range(ND):
                        if d in (0, 4):
                            h = d // 4
                            if b == 0:
                                tensor.wait_ge(s_ve, VE[("wlo", "whg", j, h)])
                            elif h == 0:
                                tensor.wait_ge(
                                    q_w3["whg"][(b * NE + j) % 3],
                                    TH3[("whg", b, j)],
                                )
                        g3 = (b * NE + j) % 3
                        last = passes3(
                            bank,
                            whi["whg"][g3][:, d * 128:(d + 1) * 128],
                            wlo["whg"][g3][:, d * 128:(d + 1) * 128],
                            xnhi[d][:], xnlo[d][:],
                            first=(d == 0), final=(d == ND - 1),
                        )
                    last.then_inc(s_mm, 1)
                    seq += 1
                for i in range(NI):
                    grp_free(tensor, b, seq)
                    bank = pm[seq % 2]
                    last = None
                    for d in range(ND):
                        if d in (0, 4):
                            h = d // 4
                            if b == 0:
                                tensor.wait_ge(s_ve, VE[("wlo", "w1", i, h)])
                            elif h == 0:
                                tensor.wait_ge(
                                    q_w3["w1"][(b * NI + i) % 3],
                                    TH3[("w1", b, i)],
                                )
                        g3 = (b * NI + i) % 3
                        last = passes3(
                            bank,
                            whi["w1"][g3][:, d * 128:(d + 1) * 128],
                            wlo["w1"][g3][:, d * 128:(d + 1) * 128],
                            xnhi[d][:], xnlo[d][:],
                            first=(d == 0), final=(d == ND - 1),
                        )
                    last.then_inc(s_mm, 1)
                    seq += 1
                for o in range(NO):
                    grp_free(tensor, b, seq)
                    bank = pm[seq % 2]
                    for u4 in range(4):
                        u = o * 4 + u4
                        tensor.wait_ge(s_sc, SC[("h1", b, u4 * 8 + 7)])
                        tensor.wait_ge(s_ve, VE[("h1lo", b, u4 * 8 + 7)])
                        last = None
                        for ii in range(8):
                            i = u4 * 8 + ii
                            if ii in (0, 4):
                                h = ii // 4
                                if b == 0:
                                    tensor.wait_ge(s_ve, VE[("wlo", "w2", u, h)])
                                elif h == 0:
                                    tensor.wait_ge(
                                        q_w3["w2"][(b * NW2U + u) % 3],
                                        TH3[("w2", b, u)],
                                    )
                            g3 = (b * NW2U + u) % 3
                            last = passes3(
                                bank,
                                whi["w2"][g3][:, ii * 128:(ii + 1) * 128],
                                wlo["w2"][g3][:, ii * 128:(ii + 1) * 128],
                                h1hi[i][:], h1lo[i][:],
                                first=(i == 0), final=(i == NI - 1),
                            )
                        last.then_inc(s_mm, 1)
                    seq += 1
                for k in range(NT):
                    for h in range(2):
                        tensor.wait_ge(s_ve, VE[("y", b, h * 4 + 3)])
                        if k >= 1:
                            tensor.wait_ge(s_ve, VE[("res", b, k - 1)])
                        elif b >= 1:
                            tensor.wait_ge(s_ve, VE[("res", b - 1, NT - 1)])
                        last = None
                        for dd in range(4):
                            d = h * 4 + dd
                            last = tensor.matmul(
                                py[h][:, dd * 128:(dd + 1) * 128],
                                s_t[d][:, k * 128:(k + 1) * 128],
                                ident[:], start=True, stop=True,
                            )
                        last.then_inc(s_mm, 1)
                if b + 1 < NB:
                    tr_group(b + 1)

        @block.scalar
        def _(scalar):
            scalar.wait_ge(q_c, 48)
            for b in range(NB):
                for tag, pl, tgt in anchors(b):
                    if tag == "mix":
                        kk = pl
                        if kk >= 1:
                            scalar.wait_ge(s_ve, VE[("scan", b, kk - 1)])
                        elif b > 0:
                            scalar.wait_ge(s_ve, VE[("scan", b - 1, ND - 1)])
                        scalar.wait_ge(s_mm, MM[("hg", b, 2 * kk)])
                        gate_bank = pm[(2 * kk) % 2]
                        scalar.activation(
                            out=zbuf[:], in_=gate_bank[:], func=AF.Sigmoid,
                        ).then_inc(s_sc, 1)
                        scalar.wait_ge(s_ve, VE[("lin", b, kk)])
                        scalar.activation(
                            out=gbuf[:], in_=lbuf[:], func=AF.Sigmoid,
                            bias=neghalf[:], scale=1.0,
                        ).then_inc(s_sc, 1)
                    elif tag == "h1":
                        i = pl
                        scalar.wait_ge(s_mm, MM[("f1", b, i)])
                        bank = pm[(NE + i) % 2]
                        scalar.activation(
                            out=h1hi[i][:], in_=bank[:], func=AF.Relu,
                            bias=b1t[:, i:i + 1], scale=1.0,
                        )
                        if i >= 1:
                            scalar.wait_ge(s_ve, VE[("h1lo", b, i - 1)])
                        elif b > 0:
                            scalar.wait_ge(s_ve, VE[("h1lo", b - 1, NI - 1)])
                        scalar.activation(
                            out=h1f[:], in_=bank[:], func=AF.Relu,
                            bias=b1t[:, i:i + 1], scale=1.0,
                        ).then_inc(s_sc, 1)
                    else:
                        o = pl
                        scalar.wait_ge(s_ve, VE[("fb", b, o)])
                        scalar.activation(
                            out=sfb[:], in_=g2b[:], func=AF.Sigmoid,
                        ).then_inc(s_sc, 1)

        @block.vector
        def _(vector):
            def vdrain():
                vector.drain()

            vector.memset(carry[:], 0.0)
            vector.memset(neghalf[:], -0.5).then_inc(s_ve, 1)

            cur = [0]

            def ve_split(target):
                while cur[0] < min(target, NALLU):
                    name, u = ALLU[cur[0]]
                    for h in range(2):
                        g = 2 * u + h
                        vector.wait_ge(
                            q_w[name][g % WFN[name]], QWT[(name, u, h)]
                        )
                        if h == 0 and u >= 3:
                            vector.wait_ge(s_mm, MM[(MMKEY[name], 0, u - 3)])
                            vector.wait_ge(
                                q_scrw[name][u % 3], SCW[(name, u - 3)]
                            )
                        vector.tensor_copy(
                            whi[name][u % 3][:, h * 512:(h + 1) * 512],
                            wf[name][g % WFN[name]][:],
                        ).then_inc(s_ve, 1)
                        vdrain()
                        vector.tensor_tensor(
                            wlo[name][u % 3][:, h * 512:(h + 1) * 512],
                            wf[name][g % WFN[name]][:],
                            whi[name][u % 3][:, h * 512:(h + 1) * 512],
                            ALU.subtract,
                        ).then_inc(s_ve, 1)
                    cur[0] += 1

            def col4(c):
                return stats[:, c:c + 25:8]

            def ln_vector(b):
                for k in range(NT):
                    vector.wait_ge(q_xk[k], 16 * (b + 1))
                    if b >= 1:
                        vector.wait_ge(s_mm, MM[("tr", b - 1, k, 1)])
                    base = k * 8
                    vector.reduce_sum(
                        stats[:, base:base + 1], xt[b % 2][k][:], axis=AX.X
                    )
                    vector.scalar_tensor_tensor(
                        xn_tm[k][:], xt[b % 2][k][:], 1.0, xt[b % 2][k][:],
                        ALU.mult, ALU.mult,
                        accum_out=stats[:, base + 1:base + 2],
                    )
                vdrain()
                su, sq_ = col4(0), col4(1)
                mean, ex2 = col4(2), col4(3)
                w, rstd, tmp = col4(4), col4(6), col4(7)
                vector.tensor_scalar_mul(mean, su, 1.0 / D)
                vector.tensor_scalar_mul(ex2, sq_, 1.0 / D)
                vdrain()
                vector.tensor_tensor(tmp, mean, mean, ALU.mult)
                vdrain()
                vector.tensor_sub(w, ex2, tmp)
                vdrain()
                vector.tensor_scalar_add(w, w, EPS)
                vdrain()
                vector.reciprocal(rstd, w)
                vdrain()
                for _ in range(7):
                    vector.tensor_mul(tmp, w, rstd)
                    vdrain()
                    vector.tensor_mul(tmp, tmp, rstd)
                    vdrain()
                    vector.tensor_scalar(tmp, tmp, -0.5, 1.5, ALU.mult, ALU.add)
                    vdrain()
                    vector.tensor_mul(rstd, rstd, tmp)
                    vdrain()
                for k in range(NT):
                    base = k * 8
                    vector.tensor_scalar(
                        xn_tm[k][:], xt[b % 2][k][:],
                        stats[:, base + 2:base + 3], stats[:, base + 6:base + 7],
                        ALU.subtract, ALU.mult,
                    ).then_inc(s_ve, 1)

            ln_vector(0)
            for b in range(NB):
                for k in range(NT):
                    for h in range(2):
                        vector.wait_ge(s_mm, MM[("tr", b, k, h)])
                        for dd in range(4):
                            d = h * 4 + dd
                            vector.tensor_copy(
                                xnhi[d][:, k * 128:(k + 1) * 128],
                                pt[h][:, dd * 128:(dd + 1) * 128],
                            )
                        vdrain()
                        last = None
                        for dd in range(4):
                            d = h * 4 + dd
                            last = vector.tensor_tensor(
                                xnlo[d][:, k * 128:(k + 1) * 128],
                                pt[h][:, dd * 128:(dd + 1) * 128],
                                xnhi[d][:, k * 128:(k + 1) * 128],
                                ALU.subtract,
                            )
                        last.then_inc(s_ve, 1)
                    if b == 0:
                        ve_split(k)
                for tag, pl, tgt in anchors(b):
                    if tag == "mix":
                        kk = pl
                        if b > 0 and kk == 0:
                            vector.wait_ge(s_mm, MM[("ytr", b - 1, NT - 1, 1)])
                        vector.wait_ge(s_mm, MM[("hg", b, 2 * kk + 1)])
                        hid_bank = pm[(2 * kk + 1) % 2]
                        vector.tensor_scalar_add(
                            lbuf[:], hid_bank[:], 0.5
                        ).then_inc(s_ve, 1)
                        vector.wait_ge(s_sc, SC[("sigh", b, kk)])
                        vdrain()
                        vector.tensor_max(lbuf[:], lbuf[:], gbuf[:])
                        vector.wait_ge(s_sc, SC[("z", b, kk)])
                        vdrain()
                        vector.tensor_mul(vbuf[:], zbuf[:], lbuf[:])
                        vector.tensor_scalar(
                            zbuf[:], zbuf[:], -1.0, 1.0, ALU.mult, ALU.add
                        )
                        vdrain()
                        vector.tensor_tensor_scan(
                            s_t[kk][:], zbuf[:], vbuf[:],
                            carry[:, kk:kk + 1], ALU.mult, ALU.add,
                        )
                        vdrain()
                        vector.tensor_copy(
                            carry[:, kk:kk + 1], s_t[kk][:, BT - 1:BT]
                        ).then_inc(s_ve, 1)
                    elif tag == "h1":
                        i = pl
                        vector.wait_ge(s_sc, SC[("h1", b, i)])
                        vector.tensor_tensor(
                            h1lo[i][:], h1f[:], h1hi[i][:], ALU.subtract
                        ).then_inc(s_ve, 1)
                        if i == NI - 1 and b + 1 < NB:
                            ln_vector(b + 1)
                    else:
                        o = pl
                        vector.wait_ge(s_mm, MM[("f2u", b, o * 4 + 3)])
                        bank = pm[(NE + NI + o) % 2]
                        vector.tensor_scalar_add(
                            g2b[:], bank[:], b2t[:, o:o + 1]
                        ).then_inc(s_ve, 1)
                        vector.wait_ge(s_sc, SC[("sf", b, o)])
                        vdrain()
                        vector.tensor_mul(
                            g2b[:], g2b[:], sfb[:]
                        ).then_inc(s_ve, 1)
                        vdrain()
                        vector.tensor_mul(
                            s_t[o][:], s_t[o][:], g2b[:]
                        ).then_inc(s_ve, 1)
                    ve_split(tgt)
                for k in range(NT):
                    last = None
                    for h in range(2):
                        vector.wait_ge(s_mm, MM[("ytr", b, k, h)])
                        last = vector.tensor_tensor(
                            xt[b % 2][k][:, h * 512:(h + 1) * 512],
                            xt[b % 2][k][:, h * 512:(h + 1) * 512],
                            py[h][:], ALU.add,
                        )
                    last.then_inc(s_ve, 1)

    return nc


_NC_CACHE = None


def kernel(L0_attn_x, L0_attn_W_hg, L0_ffn_W1, L0_ffn_b1, L0_ffn_W2, L0_ffn_b2):
    global _NC_CACHE
    x = np.ascontiguousarray(np.asarray(L0_attn_x, dtype=np.float32))
    whg = np.asarray(L0_attn_W_hg, dtype=np.float32)
    w1 = np.asarray(L0_ffn_W1, dtype=np.float32)
    b1 = np.asarray(L0_ffn_b1, dtype=np.float32)
    w2 = np.asarray(L0_ffn_W2, dtype=np.float32)
    b2 = np.asarray(L0_ffn_b2, dtype=np.float32)

    def halves(units):
        n = units.shape[0]
        return np.ascontiguousarray(
            units.reshape(n, 128, 2, 512).transpose(0, 2, 1, 3).reshape(2 * n, 128, 512)
        )

    e_order = []
    for kk in range(8):
        e_order.extend([8 + kk, kk])
    whg_t = whg.reshape(8, 128, 16, 128).transpose(2, 1, 0, 3)[e_order]
    whg_t = halves(whg_t.reshape(16, 128, 1024))
    w1_t = halves(
        w1.reshape(8, 128, 32, 128).transpose(2, 1, 0, 3).reshape(32, 128, 1024)
    )
    w2_t = halves(
        w2.reshape(4, 8, 128, 8, 128).transpose(3, 0, 2, 1, 4).reshape(32, 128, 1024)
    )
    b1_t = np.ascontiguousarray(b1.reshape(32, 128).T)
    b2_t = np.ascontiguousarray(b2.reshape(8, 128).T)
    ident = np.eye(128, dtype=np.float32)

    if _NC_CACHE is None:
        _NC_CACHE = build_nc()
    nc = _NC_CACHE

    in_maps = [
        {
            "x": np.ascontiguousarray(x[c]),
            "whg": whg_t,
            "w1": w1_t,
            "w2": w2_t,
            "b1t": b1_t,
            "b2t": b2_t,
            "ident": ident,
        }
        for c in range(8)
    ]
    res = run_bass_kernel_spmd(nc, in_maps, core_ids=list(range(8)))
    return np.stack([r["out"] for r in res.results], axis=0)


# version 20 (final)
# speedup vs baseline: 1.0395x; 1.0111x over previous
import numpy as np
from contextlib import ExitStack

import concourse.bass as bass
import concourse.mybir as mybir
from concourse.bass_utils import run_bass_kernel_spmd

F32 = mybir.dt.float32
F16 = mybir.dt.float16
I32 = mybir.dt.int32
AF = mybir.ActivationFunctionType
ALU = mybir.AluOpType
AX = mybir.AxisListType

S = 2048
D = 1024
NB = 4
BT = 512
NT = 4
ND = 8
NE = 16
NI = 32
NO = 8
NW2U = 32
NUNITS = {"whg": NE, "w1": NI, "w2": NW2U}
MMKEY = {"whg": "hg", "w1": "f1", "w2": "f2u"}
WFN = {"whg": 2, "w1": 3, "w2": 3}
EPS = 1e-5


class Ctr:

    def __init__(self):
        self.n = 0
        self.at = {}

    def ev(self, key, k=1):
        self.n += k
        self.at[key] = self.n
        return self.n


def plan():
    mm = Ctr()
    ve = Ctr()
    sc = Ctr()
    gp = Ctr()

    qwt = {}
    for name, nu in NUNITS.items():
        for u in range(nu):
            for h in range(2):
                g = 2 * u + h
                qwt[(name, u, h)] = 16 * (g // WFN[name] + 1)

    def tr_events(b):
        for k in range(NT):
            for h in range(2):
                mm.ev(("tr", b, k, h))

    tr_events(0)
    for b in range(NB):
        for j in range(NE):
            mm.ev(("hg", b, j))
        for i in range(NI):
            mm.ev(("f1", b, i))
        for u in range(NW2U):
            mm.ev(("f2u", b, u))
        for k in range(NT):
            for h in range(2):
                mm.ev(("ytr", b, k, h))
        if b + 1 < NB:
            tr_events(b + 1)

    ALLU = []
    for name, nu in NUNITS.items():
        for u in range(nu):
            ALLU.append((name, u))
    NALLU = len(ALLU)

    def anchors(b):
        out = []
        for kk in range(ND):
            out.append(("mix", kk, 2 * kk + 6))
        for i in range(NI):
            out.append(("h1", i, NE + i + 5))
        for o in range(NO):
            out.append(("f2", o, NE + NI + 4 * (o + 1) + 5))
        if b > 0:
            out = [(t, p, 0) for (t, p, _) in out]
        return out

    scw = {}
    cnt3 = {}
    for name, u in ALLU:
        s = u % 3
        cnt3[(name, s)] = cnt3.get((name, s), 0) + 1
        scw[(name, u)] = 32 * cnt3[(name, s)]

    th3 = {}
    cnt3b = {}
    for b in range(1, NB):
        for name, nu in NUNITS.items():
            for u in range(nu):
                s = (b * nu + u) % 3
                cnt3b[(name, s)] = cnt3b.get((name, s), 0) + 1
                th3[(name, b, u)] = 32 * cnt3b[(name, s)]

    sc.ev(("rstd", 0))
    for b in range(NB):
        for tag, pl, tgt in anchors(b):
            if tag == "mix":
                sc.ev(("z", b, pl))
                sc.ev(("sigh", b, pl))
            elif tag == "h1":
                sc.ev(("h1", b, pl))
                if pl == NI - 1 and b + 1 < NB:
                    sc.ev(("rstd", b + 1))
            else:
                sc.ev(("sf", b, pl))

    ve.ev(("init",))
    cur_ve = [0]

    def ve_split(target):
        while cur_ve[0] < min(target, NALLU):
            name, u = ALLU[cur_ve[0]]
            for h in range(2):
                ve.ev(("whi", name, u, h))
                ve.ev(("wlo", name, u, h))
            cur_ve[0] += 1

    def ln_events(b):
        ve.ev(("lnrw", b))
        for k in range(NT):
            ve.ev(("xn", b, k))

    ln_events(0)
    for b in range(NB):
        for k in range(NT):
            for h in range(2):
                ve.ev(("evac", b, k, h))
            if b == 0:
                ve_split(k)
        for tag, pl, tgt in anchors(b):
            if tag == "mix":
                ve.ev(("lin", b, pl))
                ve.ev(("scan", b, pl))
            elif tag == "h1":
                ve.ev(("h1lo", b, pl))
                if pl == NI - 1 and b + 1 < NB:
                    ln_events(b + 1)
            else:
                ve.ev(("fb", b, pl))
                ve.ev(("g2", b, pl))
                ve.ev(("y", b, pl))
            ve_split(tgt)
        for k in range(NT):
            ve.ev(("res", b, k))

    return mm, ve, sc, gp, qwt, ALLU, anchors, scw, th3


def build_nc():
    mm, ve, sc, gp, qwt, ALLU, anchors, SCW, TH3 = plan()
    MM, VE, SC, QWT = mm.at, ve.at, sc.at, qwt
    NALLU = len(ALLU)

    nc = bass.Bass("TRN2", target_bir_lowering=False,
                   detect_race_conditions=False)

    x_hbm = nc.dram_tensor("x", [S, D], F32, kind="ExternalInput").ap()
    whg_hbm = nc.dram_tensor("whg", [2 * NE, 128, 512], F32, kind="ExternalInput").ap()
    w1_hbm = nc.dram_tensor("w1", [2 * NI, 128, 512], F32, kind="ExternalInput").ap()
    w2_hbm = nc.dram_tensor("w2", [2 * NW2U, 128, 512], F32, kind="ExternalInput").ap()
    b1_hbm = nc.dram_tensor("b1t", [128, NI], F32, kind="ExternalInput").ap()
    b2_hbm = nc.dram_tensor("b2t", [128, NO], F32, kind="ExternalInput").ap()
    id_hbm = nc.dram_tensor("ident", [128, 128], F32, kind="ExternalInput").ap()
    out_hbm = nc.dram_tensor("out", [S, D], F32, kind="ExternalOutput").ap()
    whbm = {"whg": whg_hbm, "w1": w1_hbm, "w2": w2_hbm}
    scr_hi = {n: nc.dram_tensor(f"scr_hi_{n}", [nu, 128, 1024], F16).ap()
              for n, nu in NUNITS.items()}
    scr_lo = {n: nc.dram_tensor(f"scr_lo_{n}", [nu, 128, 1024], F16).ap()
              for n, nu in NUNITS.items()}

    es = ExitStack()
    with es:
        def sb(name, shape, dt=F32):
            return es.enter_context(nc.sbuf_tensor(name, shape, dt))

        def ps(name, shape):
            return es.enter_context(nc.psum_tensor(name, shape, F32))

        xt = [[sb(f"xt{p}_{k}", [128, D]) for k in range(NT)] for p in range(2)]
        xn_tm = [sb(f"xntm{k}", [128, D]) for k in range(NT)]
        xnhi = [sb(f"xnhi{d}", [128, BT], F16) for d in range(ND)]
        xnlo = [sb(f"xnlo{d}", [128, BT], F16) for d in range(ND)]
        h1hi = [sb(f"h1hi{i}", [128, BT], F16) for i in range(NI)]
        h1lo = [sb(f"h1lo{i}", [128, BT], F16) for i in range(NI)]
        h1f = sb("h1f", [128, BT])
        s_t = [sb(f"s{d}", [128, BT]) for d in range(ND)]
        zbuf = sb("zbuf", [128, BT])
        gbuf = sb("gbuf", [128, BT])
        lbuf = sb("lbuf", [128, BT])
        vbuf = sb("vbuf", [128, BT])
        sfb = lbuf
        g2b = vbuf
        wf = {n: [sb(f"wf_{n}{s}", [128, 512]) for s in range(WFN[n])]
              for n in NUNITS}
        whi = {n: [sb(f"whi_{n}{p}", [128, 1024], F16) for p in range(3)]
               for n in NUNITS}
        wlo = {n: [sb(f"wlo_{n}{p}", [128, 1024], F16) for p in range(3)]
               for n in NUNITS}
        stats = sb("stats", [128, NT * 8])
        carry = sb("carry", [128, ND])
        neghalf = sb("neghalf", [128, 1])
        b1t = sb("b1t_s", [128, NI])
        b2t = sb("b2t_s", [128, NO])
        ident = sb("ident_s", [128, 128])

        pm = [ps("pm0", [128, BT]), ps("pm1", [128, BT])]
        pt = [ps("pt0", [128, BT]), ps("pt1", [128, BT])]
        py = [ps("py0", [128, BT]), ps("py1", [128, BT])]

        s_mm = es.enter_context(nc.semaphore("s_mm"))
        s_ve = es.enter_context(nc.semaphore("s_ve"))
        s_sc = es.enter_context(nc.semaphore("s_sc"))
        s_gp = es.enter_context(nc.semaphore("s_gp"))
        q_x = es.enter_context(nc.semaphore("q_x"))
        q_c = es.enter_context(nc.semaphore("q_c"))
        q_xk = [es.enter_context(nc.semaphore(f"q_xk{k}")) for k in range(NT)]
        q_w = {
            name: [es.enter_context(nc.semaphore(f"qw_{name}{s}"))
                   for s in range(WFN[name])]
            for name in NUNITS
        }
        q_scrw = {
            name: [es.enter_context(nc.semaphore(f"qsw_{name}{s}")) for s in range(3)]
            for name in NUNITS
        }
        q_w3 = {
            name: [es.enter_context(nc.semaphore(f"qw3_{name}{s}")) for s in range(3)]
            for name in NUNITS
        }
        block = es.enter_context(nc.Block())


        def grp_free(eng, b, seq):
            p = seq - 2
            if p < 0:
                return
            if p < NE:
                kk = p // 2
                if p % 2 == 0:
                    eng.wait_ge(s_sc, SC[("z", b, kk)])
                else:
                    eng.wait_ge(s_ve, VE[("lin", b, kk)])
            elif p < NE + NI:
                eng.wait_ge(s_sc, SC[("h1", b, p - NE)])
            else:
                eng.wait_ge(s_ve, VE[("fb", b, p - NE - NI)])

        @block.sync
        def _(sync):
            def wdma(name, u):
                for h in range(2):
                    g = 2 * u + h
                    if g >= WFN[name]:
                        pu, ph = divmod(g - WFN[name], 2)
                        sync.wait_ge(s_ve, VE[("wlo", name, pu, ph)])
                    sync.dma_start(
                        out=wf[name][g % WFN[name]][:], in_=whbm[name][g]
                    ).then_inc(q_w[name][g % WFN[name]], 16)

            sync.wait_ge(q_xk[NT - 1], 16)
            for name in NUNITS:
                wdma(name, 0)
            for name, u in ALLU:
                if u == 0:
                    continue
                wdma(name, u)
            for b in range(1, NB):
                for name, u in ALLU:
                    nu = NUNITS[name]
                    g = b * nu + u
                    pb, pu = divmod(g - 3, nu)
                    sync.wait_ge(s_mm, MM[(MMKEY[name], pb, pu)])
                    if b == 1:
                        sync.wait_ge(q_scrw[name][u % 3], SCW[(name, u)])
                    sync.dma_start(
                        out=whi[name][g % 3][:], in_=scr_hi[name][u]
                    ).then_inc(q_w3[name][g % 3], 16)
                    sync.dma_start(
                        out=wlo[name][g % 3][:], in_=scr_lo[name][u]
                    ).then_inc(q_w3[name][g % 3], 16)

        @block.gpsimd
        def _(gpsimd):
            gpsimd.dma_start(out=ident[:], in_=id_hbm[:]).then_inc(q_c, 16)
            gpsimd.dma_start(out=b1t[:], in_=b1_hbm[:]).then_inc(q_c, 16)
            gpsimd.dma_start(out=b2t[:], in_=b2_hbm[:]).then_inc(q_c, 16)

            def xload(b, k):
                t0 = b * BT + k * 128
                gpsimd.dma_start(
                    out=xt[b % 2][k][:], in_=x_hbm[t0:t0 + 128, :]
                ).then_inc(q_xk[k], 16)

            for k in range(NT):
                xload(0, k)
            gpsimd.wait_ge(q_xk[NT - 1], 16)
            for k in range(NT):
                xload(1, k)
            for name, u in ALLU:
                g3 = u % 3
                gpsimd.wait_ge(s_ve, VE[("wlo", name, u, 1)])
                gpsimd.dma_start(
                    out=scr_hi[name][u], in_=whi[name][g3][:]
                ).then_inc(q_scrw[name][g3], 16)
                gpsimd.dma_start(
                    out=scr_lo[name][u], in_=wlo[name][g3][:]
                ).then_inc(q_scrw[name][g3], 16)
            for b in range(NB):
                for k in range(NT):
                    t0 = b * BT + k * 128
                    gpsimd.wait_ge(s_ve, VE[("res", b, k)])
                    gpsimd.dma_start(
                        out=out_hbm[t0:t0 + 128, :], in_=xt[b % 2][k][:]
                    ).then_inc(q_x, 16)
                if b + 2 < NB:
                    for k in range(NT):
                        xload(b + 2, k)
            gpsimd.wait_ge(q_x, 16 * NB * NT)

        @block.tensor
        def _(tensor):
            tensor.wait_ge(q_c, 48)

            def tr_group(b):
                for k in range(NT):
                    slot = k
                    tensor.wait_ge(s_ve, VE[("xn", b, k)])
                    for h in range(2):
                        if k >= 1:
                            tensor.wait_ge(s_ve, VE[("evac", b, k - 1, h)])
                        elif b >= 1:
                            tensor.wait_ge(s_ve, VE[("evac", b - 1, NT - 1, h)])
                        last = None
                        for dd in range(4):
                            d = h * 4 + dd
                            last = tensor.matmul(
                                pt[h][:, dd * 128:(dd + 1) * 128],
                                xn_tm[slot][:, d * 128:(d + 1) * 128],
                                ident[:], start=True, stop=True,
                            )
                        last.then_inc(s_mm, 1)

            def passes3(bank, w_hi, w_lo, x_hi, x_lo, first, final):
                n = 0
                last = None
                for a, bb in ((w_hi, x_hi), (w_hi, x_lo), (w_lo, x_hi)):
                    last = tensor.matmul(
                        bank[:], a, bb,
                        start=(first and n == 0), stop=(final and n == 2),
                    )
                    n += 1
                return last

            tr_group(0)
            for b in range(NB):
                seq = 0
                for j in range(NE):
                    if j == 0:
                        tensor.wait_ge(s_ve, VE[("evac", b, NT - 1, 1)])
                    grp_free(tensor, b, seq)
                    bank = pm[seq % 2]
                    last = None
                    for d in range(ND):
                        if d in (0, 4):
                            h = d // 4
                            if b == 0:
                                tensor.wait_ge(s_ve, VE[("wlo", "whg", j, h)])
                            elif h == 0:
                                tensor.wait_ge(
                                    q_w3["whg"][(b * NE + j) % 3],
                                    TH3[("whg", b, j)],
                                )
                        g3 = (b * NE + j) % 3
                        last = passes3(
                            bank,
                            whi["whg"][g3][:, d * 128:(d + 1) * 128],
                            wlo["whg"][g3][:, d * 128:(d + 1) * 128],
                            xnhi[d][:], xnlo[d][:],
                            first=(d == 0), final=(d == ND - 1),
                        )
                    last.then_inc(s_mm, 1)
                    seq += 1
                for i in range(NI):
                    grp_free(tensor, b, seq)
                    bank = pm[seq % 2]
                    last = None
                    for d in range(ND):
                        if d in (0, 4):
                            h = d // 4
                            if b == 0:
                                tensor.wait_ge(s_ve, VE[("wlo", "w1", i, h)])
                            elif h == 0:
                                tensor.wait_ge(
                                    q_w3["w1"][(b * NI + i) % 3],
                                    TH3[("w1", b, i)],
                                )
                        g3 = (b * NI + i) % 3
                        last = passes3(
                            bank,
                            whi["w1"][g3][:, d * 128:(d + 1) * 128],
                            wlo["w1"][g3][:, d * 128:(d + 1) * 128],
                            xnhi[d][:], xnlo[d][:],
                            first=(d == 0), final=(d == ND - 1),
                        )
                    last.then_inc(s_mm, 1)
                    seq += 1
                for o in range(NO):
                    grp_free(tensor, b, seq)
                    bank = pm[seq % 2]
                    for u4 in range(4):
                        u = o * 4 + u4
                        tensor.wait_ge(s_sc, SC[("h1", b, u4 * 8 + 7)])
                        tensor.wait_ge(s_ve, VE[("h1lo", b, u4 * 8 + 7)])
                        last = None
                        for ii in range(8):
                            i = u4 * 8 + ii
                            if ii in (0, 4):
                                h = ii // 4
                                if b == 0:
                                    tensor.wait_ge(s_ve, VE[("wlo", "w2", u, h)])
                                elif h == 0:
                                    tensor.wait_ge(
                                        q_w3["w2"][(b * NW2U + u) % 3],
                                        TH3[("w2", b, u)],
                                    )
                            g3 = (b * NW2U + u) % 3
                            last = passes3(
                                bank,
                                whi["w2"][g3][:, ii * 128:(ii + 1) * 128],
                                wlo["w2"][g3][:, ii * 128:(ii + 1) * 128],
                                h1hi[i][:], h1lo[i][:],
                                first=(i == 0), final=(i == NI - 1),
                            )
                        last.then_inc(s_mm, 1)
                    seq += 1
                for k in range(NT):
                    for h in range(2):
                        tensor.wait_ge(s_ve, VE[("y", b, h * 4 + 3)])
                        if k >= 1:
                            tensor.wait_ge(s_ve, VE[("res", b, k - 1)])
                        elif b >= 1:
                            tensor.wait_ge(s_ve, VE[("res", b - 1, NT - 1)])
                        last = None
                        for dd in range(4):
                            d = h * 4 + dd
                            last = tensor.matmul(
                                py[h][:, dd * 128:(dd + 1) * 128],
                                s_t[d][:, k * 128:(k + 1) * 128],
                                ident[:], start=True, stop=True,
                            )
                        last.then_inc(s_mm, 1)
                if b + 1 < NB:
                    tr_group(b + 1)

        @block.scalar
        def _(scalar):
            scalar.preload_act_table(AF.Sqrt)
            scalar.wait_ge(q_c, 48)

            def ln_rstd(b):
                scalar.wait_ge(s_ve, VE[("lnrw", b)])
                scalar.activation(
                    out=stats[:, 6:31:8], in_=stats[:, 5:30:8], func=AF.Sqrt,
                ).then_inc(s_sc, 1)
                scalar.preload_act_table(AF.Sigmoid)

            ln_rstd(0)
            for b in range(NB):
                for tag, pl, tgt in anchors(b):
                    if tag == "mix":
                        kk = pl
                        if kk >= 1:
                            scalar.wait_ge(s_ve, VE[("scan", b, kk - 1)])
                        elif b > 0:
                            scalar.wait_ge(s_ve, VE[("scan", b - 1, ND - 1)])
                        scalar.wait_ge(s_mm, MM[("hg", b, 2 * kk)])
                        gate_bank = pm[(2 * kk) % 2]
                        scalar.activation(
                            out=zbuf[:], in_=gate_bank[:], func=AF.Sigmoid,
                        ).then_inc(s_sc, 1)
                        scalar.wait_ge(s_ve, VE[("lin", b, kk)])
                        scalar.activation(
                            out=gbuf[:], in_=lbuf[:], func=AF.Sigmoid,
                            bias=neghalf[:], scale=1.0,
                        ).then_inc(s_sc, 1)
                    elif tag == "h1":
                        i = pl
                        scalar.wait_ge(s_mm, MM[("f1", b, i)])
                        bank = pm[(NE + i) % 2]
                        scalar.activation(
                            out=h1hi[i][:], in_=bank[:], func=AF.Relu,
                            bias=b1t[:, i:i + 1], scale=1.0,
                        )
                        if i >= 1:
                            scalar.wait_ge(s_ve, VE[("h1lo", b, i - 1)])
                        elif b > 0:
                            scalar.wait_ge(s_ve, VE[("h1lo", b - 1, NI - 1)])
                        scalar.activation(
                            out=h1f[:], in_=bank[:], func=AF.Relu,
                            bias=b1t[:, i:i + 1], scale=1.0,
                        ).then_inc(s_sc, 1)
                        if i == NI - 1 and b + 1 < NB:
                            scalar.preload_act_table(AF.Sqrt)
                            ln_rstd(b + 1)
                    else:
                        o = pl
                        scalar.wait_ge(s_ve, VE[("fb", b, o)])
                        scalar.activation(
                            out=sfb[:], in_=g2b[:], func=AF.Sigmoid,
                        ).then_inc(s_sc, 1)

        @block.vector
        def _(vector):
            def vdrain():
                vector.drain()

            vector.memset(carry[:], 0.0)
            vector.memset(neghalf[:], -0.5).then_inc(s_ve, 1)

            cur = [0]

            def ve_split(target):
                while cur[0] < min(target, NALLU):
                    name, u = ALLU[cur[0]]
                    for h in range(2):
                        g = 2 * u + h
                        vector.wait_ge(
                            q_w[name][g % WFN[name]], QWT[(name, u, h)]
                        )
                        if h == 0 and u >= 3:
                            vector.wait_ge(s_mm, MM[(MMKEY[name], 0, u - 3)])
                            vector.wait_ge(
                                q_scrw[name][u % 3], SCW[(name, u - 3)]
                            )
                        vector.tensor_copy(
                            whi[name][u % 3][:, h * 512:(h + 1) * 512],
                            wf[name][g % WFN[name]][:],
                        ).then_inc(s_ve, 1)
                        vdrain()
                        vector.tensor_tensor(
                            wlo[name][u % 3][:, h * 512:(h + 1) * 512],
                            wf[name][g % WFN[name]][:],
                            whi[name][u % 3][:, h * 512:(h + 1) * 512],
                            ALU.subtract,
                        ).then_inc(s_ve, 1)
                    cur[0] += 1

            def col4(c):
                return stats[:, c:c + 25:8]

            def ln_vector(b):
                for k in range(NT):
                    vector.wait_ge(q_xk[k], 16 * (b + 1))
                    if b >= 1:
                        vector.wait_ge(s_mm, MM[("tr", b - 1, k, 1)])
                    base = k * 8
                    vector.reduce_sum(
                        stats[:, base:base + 1], xt[b % 2][k][:], axis=AX.X
                    )
                    vector.scalar_tensor_tensor(
                        xn_tm[k][:], xt[b % 2][k][:], 1.0, xt[b % 2][k][:],
                        ALU.mult, ALU.mult,
                        accum_out=stats[:, base + 1:base + 2],
                    )
                vdrain()
                su, sq_ = col4(0), col4(1)
                mean, nmsq = col4(2), col4(3)
                w, rw = col4(4), col4(5)
                vector.tensor_scalar(w, sq_, 1.0 / D, EPS, ALU.mult, ALU.add)
                vector.tensor_scalar_mul(mean, su, 1.0 / D)
                vdrain()
                vector.scalar_tensor_tensor(
                    nmsq, mean, -1.0, mean, ALU.mult, ALU.mult
                )
                vdrain()
                vector.tensor_tensor(w, w, nmsq, ALU.add)
                vdrain()
                vector.reciprocal(rw, w).then_inc(s_ve, 1)
                vector.wait_ge(s_sc, SC[("rstd", b)])
                for k in range(NT):
                    base = k * 8
                    vector.tensor_scalar(
                        xn_tm[k][:], xt[b % 2][k][:],
                        stats[:, base + 2:base + 3], stats[:, base + 6:base + 7],
                        ALU.subtract, ALU.mult,
                    ).then_inc(s_ve, 1)

            ln_vector(0)
            for b in range(NB):
                for k in range(NT):
                    for h in range(2):
                        vector.wait_ge(s_mm, MM[("tr", b, k, h)])
                        for dd in range(4):
                            d = h * 4 + dd
                            vector.tensor_copy(
                                xnhi[d][:, k * 128:(k + 1) * 128],
                                pt[h][:, dd * 128:(dd + 1) * 128],
                            )
                        vdrain()
                        last = None
                        for dd in range(4):
                            d = h * 4 + dd
                            last = vector.tensor_tensor(
                                xnlo[d][:, k * 128:(k + 1) * 128],
                                pt[h][:, dd * 128:(dd + 1) * 128],
                                xnhi[d][:, k * 128:(k + 1) * 128],
                                ALU.subtract,
                            )
                        last.then_inc(s_ve, 1)
                    if b == 0:
                        ve_split(k)
                for tag, pl, tgt in anchors(b):
                    if tag == "mix":
                        kk = pl
                        if b > 0 and kk == 0:
                            vector.wait_ge(s_mm, MM[("ytr", b - 1, NT - 1, 1)])
                        vector.wait_ge(s_mm, MM[("hg", b, 2 * kk + 1)])
                        hid_bank = pm[(2 * kk + 1) % 2]
                        vector.tensor_scalar_add(
                            lbuf[:], hid_bank[:], 0.5
                        ).then_inc(s_ve, 1)
                        vector.wait_ge(s_sc, SC[("sigh", b, kk)])
                        vdrain()
                        vector.tensor_max(lbuf[:], lbuf[:], gbuf[:])
                        vector.wait_ge(s_sc, SC[("z", b, kk)])
                        vdrain()
                        vector.tensor_mul(vbuf[:], zbuf[:], lbuf[:])
                        vector.tensor_scalar(
                            zbuf[:], zbuf[:], -1.0, 1.0, ALU.mult, ALU.add
                        )
                        vdrain()
                        vector.tensor_tensor_scan(
                            s_t[kk][:], zbuf[:], vbuf[:],
                            carry[:, kk:kk + 1], ALU.mult, ALU.add,
                        )
                        vdrain()
                        vector.tensor_copy(
                            carry[:, kk:kk + 1], s_t[kk][:, BT - 1:BT]
                        ).then_inc(s_ve, 1)
                    elif tag == "h1":
                        i = pl
                        vector.wait_ge(s_sc, SC[("h1", b, i)])
                        vector.tensor_tensor(
                            h1lo[i][:], h1f[:], h1hi[i][:], ALU.subtract
                        ).then_inc(s_ve, 1)
                        if i == NI - 1 and b + 1 < NB:
                            ln_vector(b + 1)
                    else:
                        o = pl
                        vector.wait_ge(s_mm, MM[("f2u", b, o * 4 + 3)])
                        bank = pm[(NE + NI + o) % 2]
                        vector.tensor_scalar_add(
                            g2b[:], bank[:], b2t[:, o:o + 1]
                        ).then_inc(s_ve, 1)
                        vector.wait_ge(s_sc, SC[("sf", b, o)])
                        vdrain()
                        vector.tensor_mul(
                            g2b[:], g2b[:], sfb[:]
                        ).then_inc(s_ve, 1)
                        vdrain()
                        vector.tensor_mul(
                            s_t[o][:], s_t[o][:], g2b[:]
                        ).then_inc(s_ve, 1)
                    ve_split(tgt)
                for k in range(NT):
                    last = None
                    for h in range(2):
                        vector.wait_ge(s_mm, MM[("ytr", b, k, h)])
                        last = vector.tensor_tensor(
                            xt[b % 2][k][:, h * 512:(h + 1) * 512],
                            xt[b % 2][k][:, h * 512:(h + 1) * 512],
                            py[h][:], ALU.add,
                        )
                    last.then_inc(s_ve, 1)

    return nc


_NC_CACHE = None


def kernel(L0_attn_x, L0_attn_W_hg, L0_ffn_W1, L0_ffn_b1, L0_ffn_W2, L0_ffn_b2):
    global _NC_CACHE
    x = np.ascontiguousarray(np.asarray(L0_attn_x, dtype=np.float32))
    whg = np.asarray(L0_attn_W_hg, dtype=np.float32)
    w1 = np.asarray(L0_ffn_W1, dtype=np.float32)
    b1 = np.asarray(L0_ffn_b1, dtype=np.float32)
    w2 = np.asarray(L0_ffn_W2, dtype=np.float32)
    b2 = np.asarray(L0_ffn_b2, dtype=np.float32)

    def halves(units):
        n = units.shape[0]
        return np.ascontiguousarray(
            units.reshape(n, 128, 2, 512).transpose(0, 2, 1, 3).reshape(2 * n, 128, 512)
        )

    e_order = []
    for kk in range(8):
        e_order.extend([8 + kk, kk])
    whg_t = whg.reshape(8, 128, 16, 128).transpose(2, 1, 0, 3)[e_order]
    whg_t = halves(whg_t.reshape(16, 128, 1024))
    w1_t = halves(
        w1.reshape(8, 128, 32, 128).transpose(2, 1, 0, 3).reshape(32, 128, 1024)
    )
    w2_t = halves(
        w2.reshape(4, 8, 128, 8, 128).transpose(3, 0, 2, 1, 4).reshape(32, 128, 1024)
    )
    b1_t = np.ascontiguousarray(b1.reshape(32, 128).T)
    b2_t = np.ascontiguousarray(b2.reshape(8, 128).T)
    ident = np.eye(128, dtype=np.float32)

    if _NC_CACHE is None:
        _NC_CACHE = build_nc()
    nc = _NC_CACHE

    in_maps = [
        {
            "x": np.ascontiguousarray(x[c]),
            "whg": whg_t,
            "w1": w1_t,
            "w2": w2_t,
            "b1t": b1_t,
            "b2t": b2_t,
            "ident": ident,
        }
        for c in range(8)
    ]
    res = run_bass_kernel_spmd(nc, in_maps, core_ids=list(range(8)))
    return np.stack([r["out"] for r in res.results], axis=0)
